# version 01
# baseline (speedup 1.0000x reference)
import hashlib
import os
import shutil
import numpy as np
import ml_dtypes
import concourse.bass as bass
import concourse.mybir as mybir
import concourse.bass_utils as _bass_utils
from concourse.bass_utils import run_bass_kernel_spmd

_NEFF_CACHE_DIR = "/tmp/bass_neff_cache"
_ORIG_COMPILE = _bass_utils.compile_bir_kernel


def _cached_compile_bir_kernel(bir_json, tmpdir, neff_name="file.neff"):
    h = hashlib.sha256(bir_json).hexdigest()[:24]
    os.makedirs(_NEFF_CACHE_DIR, exist_ok=True)
    cpath = os.path.join(_NEFF_CACHE_DIR, h + ".neff")
    dst = os.path.join(tmpdir, neff_name)
    if os.path.exists(cpath):
        shutil.copyfile(cpath, dst)
        return dst
    out = _ORIG_COMPILE(bir_json, tmpdir, neff_name)
    try:
        shutil.copyfile(out, cpath + ".tmp")
        os.replace(cpath + ".tmp", cpath)
    except OSError:
        pass
    return out


_bass_utils.compile_bir_kernel = _cached_compile_bir_kernel
try:
    import concourse.bass2jax as _b2j
    _b2j.compile_bir_kernel = _cached_compile_bir_kernel
except Exception:
    pass

F32 = mybir.dt.float32
BF16 = mybir.dt.bfloat16
I32 = mybir.dt.int32
AF = mybir.ActivationFunctionType
ALU = mybir.AluOpType
AX = mybir.AxisListType

T = 2048
D = 2048
NT = T // 128
ND = D // 128
EPS = 1e-5
ATT_SCALE = 128.0 ** -0.5
HG = 4

CONST_VALUES = [0.0, 1.0, EPS, 1.0 / D, -1.0 / D, 0.5, -0.5, 512.5, 512.0,
                -512.0, 127.0, 1.0 / 127.0, -1.0, 2.0, -2.0, 1.0 / 4096.0,
                ATT_SCALE, -1e30, 1.0 / (8192.0 * 2048.0)]

N_SUBLAYERS = 4
DEBUG_TAPS = []

ENGS = ("sync", "tensor", "vector", "scalar", "gpsimd")


class Sem:
    __slots__ = ("h", "count", "name")

    def __init__(self, h, name):
        self.h = h
        self.count = 0
        self.name = name


class Prog:
    def __init__(self, nc):
        self.nc = nc
        self.q = {e: [] for e in ENGS}
        self.esem = {e: Sem(nc.alloc_semaphore(f"sem_{e}"), e) for e in ENGS}

    def emit(self, eng, build, waits=(), dma=False):
        waits = tuple((s, int(v)) for s, v in waits)
        sem = self.esem[eng]
        amt = 16 if dma else 1
        sem.count += amt
        self.q[eng].append((build, waits, ((sem, amt),)))

    def dma(self, eng, dst, src, waits=()):
        self.emit(eng, lambda e: e.dma_start(dst, src), waits, dma=True)

    def aft(self, *engs):
        return tuple((self.esem[e], self.esem[e].count) for e in engs)

    def finalize(self):
        with self.nc.Block() as blk:
            for eng in ENGS:
                ops = self.q[eng]
                if not ops:
                    continue

                def body(e, ops=ops):
                    hi = {}
                    for build, waits, incs in ops:
                        for s, v in waits:
                            if hi.get(s.name, -1) < v:
                                hi[s.name] = v
                                e.wait_ge(s.h, v)
                        inst = build(e)
                        for s, n in incs:
                            inst.then_inc(s.h, n)

                getattr(blk, eng)(body)


def build_nc():
    nc = bass.Bass("TRN2", target_bir_lowering=False)

    def din(name, shape, dtype=F32):
        return nc.dram_tensor(name, list(shape), dtype, kind="ExternalInput").ap()

    def dout(name, shape, dtype=F32):
        return nc.dram_tensor(name, list(shape), dtype, kind="ExternalOutput").ap()

    def dscr(name, shape, dtype=F32):
        return nc.dram_tensor(name, list(shape), dtype, kind="Internal").ap()

    x_in = din("x", [T, D])
    out = dout("out", [T, D])

    wqkv_blk = din("wqkv_blk", [128, 48 * D])
    wqkv_v = din("wqkv_v", [128, 12 * 16 * 512])
    wo_blk = din("wo_blk", [128, ND * D])

    H0 = 8192
    w1t_blk = din("w1t_blk", [128, (H0 // 128) * D])
    w2t_b512 = din("w2t_b512", [128, (D // 512) * (H0 // 128) * 512])
    b1_l0 = din("b1_l0", [128, H0 // 128])
    b2bc_l0 = din("b2bc_l0", [128, D])

    wip_blk = din("wip_blk", [128, 67 * D])
    wot_b512 = din("wot_b512", [128, 4 * 32 * 512])
    cw_in = din("cw_in", [128, 34 * 4])
    cb_in = din("cb_in", [128, 34])
    dtb_in = din("dtb_in", [128, 1])
    alog_in = din("alog_in", [128, 1])
    dbc_in = din("dbc_in", [128, 64])
    nw_in = din("nw_in", [128, 4096])
    onesf_in = din("onesf_in", [128, 128])
    identf_in = din("identf_in", [128, 128])
    masklo_in = din("masklo_in", [128, 128], BF16)

    w1t_l1_b512 = din("w1t_l1_b512", [128, 16 * 16 * 512])
    w2t_l1_b512 = din("w2t_l1_b512", [128, 4 * 64 * 512])

    ident = din("ident_bf16", [128, 128], BF16)
    consts = din("consts", [128, len(CONST_VALUES)])
    mask_tri = din("mask_tri", [128, 128])

    xres = [dscr("xres0", [T, D]), dscr("xres1", [T, D])]
    aout_blk = dscr("aout_blk", [128, ND * T], BF16)
    h_blk = dscr("h_blk", [128, (8192 // 128) * T], BF16)
    f_dram = dscr("f_dram", [T, D])
    gate_blk = dscr("gate_blk", [128, 32 * T], BF16)
    xbc_blk = dscr("xbc_blk", [128, 34 * T], BF16)
    y_blk = dscr("y_blk", [128, NT * 4096], BF16)
    hgn_blk = dscr("hgn_blk", [128, 32 * T], BF16)
    wq1_scr = dscr("wq1_scr", [128, 16 * 16 * 512], BF16)

    taps = {}
    for tname, shape, dt_ in [("ln0", [D, T], F32), ("x1", [T, D], F32),
                              ("x2", [T, D], F32), ("x3", [T, D], F32),
                              ("ffn0", [T, D], F32), ("h0", [128, 16 * T], F32),
                              ("l1gate", [128, 32 * T], BF16),
                              ("l1xbc", [128, 34 * T], BF16),
                              ("l1y", [128, NT * 4096], BF16),
                              ("l1dbg", [128, 12288], F32)]:
        if tname in DEBUG_TAPS:
            taps[tname] = dout("tap_" + tname, shape, dt_)

    def s_alloc(name, cols, dtype):
        return nc.alloc_sbuf_tensor(name, [128, cols], dtype).ap()

    ln_fm = s_alloc("ln_fm", ND * T, BF16)
    arena = s_alloc("arena", 32768, BF16)
    wbuf = s_alloc("wbuf", 4096, BF16)
    xt = [s_alloc(f"xt{i}", D, F32) for i in range(2)]
    lnt_ = s_alloc("lnt0", D, BF16)
    lnt = [lnt_, lnt_]
    dump = s_alloc("dump", D, F32)
    f_t = s_alloc("f_t", D, F32)
    stat = s_alloc("stat", 64, F32)
    c_all = s_alloc("c_all", len(CONST_VALUES), F32)
    id_bf = s_alloc("id_bf", 128, BF16)
    mask_sb = s_alloc("mask_sb", 128, F32)
    onesf = s_alloc("onesf", 128, F32)
    ar32 = s_alloc("ar32", 8192, F32)
    idf_sb = s_alloc("idf_sb", 128, F32)

    q_fm = arena[:, 0:HG * T]
    k_fm = arena[:, HG * T:2 * HG * T]
    v_tm = arena[:, 2 * HG * T:3 * HG * T]
    ao_st = [arena[:, 3 * HG * T:3 * HG * T + T],
             arena[:, 3 * HG * T + T:3 * HG * T + 2 * T]]
    p_bf = arena[:, 28672:30720]
    pt_bf = arena[:, 30720:32768]
    wo_sb = arena[:, 0:ND * D]
    qwb = [wbuf[:, 0:2048], wbuf[:, 0:2048]]
    kwb = [wbuf[:, 2048:4096], wbuf[:, 2048:4096]]
    vwb = wbuf[:, 0:4096]
    at_blk = [wbuf[:, 0:2048], wbuf[:, 2048:4096]]
    at_ffn = wbuf[:, 0:4096]
    w2blk = arena[:, 0:32768]
    h_st = arena[:, 28672:30720]
    p_f32 = dump
    b1_sb = ar32[:, 2048:2112]
    b2_sb = ar32[:, 0:2048]

    Z = ln_fm
    xraw_sb = Z[:, 0:8840]
    xtm_sb = Z[:, 8960:17152]
    ytm_sb = Z[:, 17152:25344]
    pbf_sb = Z[:, 25344:29440]
    bf_sb = Z[:, 29440:29696]
    cf_sb = Z[:, 29696:29952]
    btm_sb = Z[:, 29952:30208]
    mt_sb = Z[:, 30208:30592]
    xdt8_sb = Z[:, 30592:31616]
    xdd8_sb = Z[:, 31616:32640]
    masklo_sb = Z[:, 32640:32768]
    g1_st = arena[:, 28672:30720]
    x1_st = arena[:, 30720:32768]
    ya_sb = Z[:, 0:4096]
    gb_sb = Z[:, 4096:8192]
    gtm_sb = Z[:, 8192:12288]
    hx_sb = Z[:, 12288:16384]
    ncum_tm = ar32[:, 0:128]
    dt_tm = ar32[:, 128:256]
    dec_tm = ar32[:, 256:384]
    ec_tm = ar32[:, 384:512]
    eT_all = ar32[:, 512:576]
    psb = ar32[:, 576:4672]
    cw_sb = ar32[:, 4672:4808]
    cb_sb = ar32[:, 4808:4844]
    ca_sb = ar32[:, 4844:5100]
    lt_sb = ar32[:, 5100:5228]
    yo_sb = ar32[:, 5228:5356]
    dbc_sb = ar32[:, 5356:5420]
    cumdec_c = ar32[:, 5420:5932]
    mcol_area = ar32[:, 5932:5956]
    nw_sb = ar32[:, 0:4096]
    sg_sb = ar32[:, 4096:8192]

    def cc(val):
        i = CONST_VALUES.index(val)
        return c_all[:, i:i + 1]

    for _k, _v in enumerate(CONST_VALUES):
        nc.const_aps.register(F32, _v, c_all[:, _k:_k + 1])

    ps = [nc.alloc_psum_tensor(f"ps{i}", [128, 512], F32).ap() for i in range(6)]
    i32t = nc.alloc_psum_tensor("i32p", [128, 512], I32).ap()
    pt0 = nc.alloc_psum_tensor("pt0", [128, 1024], BF16).ap()
    pt = [pt0[:, 0:512], pt0[:, 512:1024]]

    P = Prog(nc)
    SY = P.esem["sync"]

    P.dma("sync", id_bf[:, :], ident)
    P.dma("sync", c_all[:, :], consts)
    P.dma("sync", mask_sb[:, :], mask_tri)
    consts_ready = SY.count
    for _eng in ("tensor", "vector", "scalar", "gpsimd"):
        P.emit(_eng, lambda e: e.nop(), ((SY, consts_ready),))

    stat_ctr = [0]

    def stat_range(n):
        b = stat_ctr[0]
        if b + n > 64:
            b = 0
        stat_ctr[0] = b + n
        return b, [stat[:, b + k:b + k + 1] for k in range(n)]

    def ln_stats(xtile, waits):
        _, (c_sum, c_nm, c_std, c_inv) = stat_range(4)
        P.emit("vector", lambda e: e.reduce_sum(c_sum, xtile, axis=AX.X), waits)
        P.emit("vector", lambda e: e.tensor_scalar(c_nm, c_sum, -1.0 / D, None,
                                                   ALU.mult), P.aft("vector"))
        P.emit("scalar", lambda e: e.activation(dump[:, :], xtile, AF.Square,
                                                bias=c_nm, scale=1.0,
                                                accum_out=c_std), P.aft("vector"))
        P.emit("scalar", lambda e: e.activation(c_std, c_std, AF.Sqrt,
                                                bias=EPS, scale=1.0 / D),
               P.aft("scalar"))
        P.emit("vector", lambda e: e.reciprocal(c_inv, c_std), P.aft("scalar"))
        return c_nm, c_inv

    def transpose_to(dst, dst_col0, src_bf, nblk, dst_stride, extra0=()):
        for j in range(nblk):
            bank = pt[j % 2]
            P.emit("tensor",
                   lambda e, j=j, bank=bank: e.transpose(
                       bank[:, 0:128], src_bf[:, j * 128:(j + 1) * 128],
                       id_bf[:, :]),
                   P.aft("vector") + (extra0 if j == 0 else ()))
            P.emit("vector",
                   lambda e, j=j, bank=bank: e.tensor_copy(
                       dst[:, dst_col0 + j * dst_stride:
                           dst_col0 + j * dst_stride + 128], bank[:, 0:128]),
                   P.aft("tensor"))

    def build_ln_fm_from_dram(src_dram):
        for i in range(NT):
            b = i % 2
            P.dma("sync", xt[b][:, :], src_dram[i * 128:(i + 1) * 128, :],
                  P.aft("vector", "scalar") if i >= 2 else ((SY, consts_ready),))
            c_nm, c_inv = ln_stats(xt[b][:, :], P.aft("sync"))
            P.emit("vector", lambda e, b=b, c_nm=c_nm, c_inv=c_inv:
                   e.tensor_scalar(lnt[b][:, :], xt[b][:, :], c_nm, c_inv,
                                   ALU.add, ALU.mult),
                   P.aft("vector", "tensor"))
            transpose_to(ln_fm, i * 128, lnt[b][:, :], ND, T)

    build_ln_fm_from_dram(x_in)

    if "ln0" in taps:
        for j in range(ND):
            P.emit("vector", lambda e, j=j: e.tensor_copy(
                dump[:, :], ln_fm[:, j * T:(j + 1) * T]), P.aft("sync", "vector"))
            P.dma("sync", taps["ln0"][j * 128:(j + 1) * 128, :], dump[:, :],
                  P.aft("vector"))

    def l0_attn():
        n_groups = 16 // HG
        for g in range(n_groups):
            for h4 in range(HG):
                h = g * HG + h4
                b = h4 % 2
                P.dma("gpsimd", qwb[b][:, :], wqkv_blk[:, h * D:(h + 1) * D],
                      P.aft("tensor"))
                P.dma("gpsimd", kwb[b][:, :],
                      wqkv_blk[:, (16 + h) * D:(17 + h) * D], P.aft("tensor"))
                ldq = P.aft("gpsimd")
                for tc in range(4):
                    for kc in range(ND):
                        P.emit("tensor", lambda e, b=b, kc=kc, tc=tc:
                               e.matmul(ps[0][:, :],
                                        qwb[b][:, kc * 128:(kc + 1) * 128],
                                        ln_fm[:, kc * T + tc * 512:kc * T + tc * 512 + 512],
                                        start=(kc == 0), stop=(kc == ND - 1)),
                               (ldq + P.aft("scalar")) if kc == 0 else ())
                    P.emit("scalar", lambda e, h4=h4, tc=tc: e.copy(
                        q_fm[:, h4 * T + tc * 512:h4 * T + tc * 512 + 512],
                        ps[0][:, :]), P.aft("tensor"))
                    for kc in range(ND):
                        P.emit("tensor", lambda e, b=b, kc=kc, tc=tc:
                               e.matmul(ps[1][:, :],
                                        kwb[b][:, kc * 128:(kc + 1) * 128],
                                        ln_fm[:, kc * T + tc * 512:kc * T + tc * 512 + 512],
                                        start=(kc == 0), stop=(kc == ND - 1)),
                               P.aft("vector") if kc == 0 else ())
                    P.emit("vector", lambda e, h4=h4, tc=tc: e.tensor_copy(
                        k_fm[:, h4 * T + tc * 512:h4 * T + tc * 512 + 512],
                        ps[1][:, :]), P.aft("tensor"))
            for vh in range(2):
                vb = 16 + 2 * g + vh
                P.dma("gpsimd", vwb[:, :],
                      wqkv_v[:, vb * 4096:(vb + 1) * 4096], P.aft("tensor"))
                ldv = P.aft("gpsimd")
                for ti in range(NT):
                    for kc in range(ND):
                        P.emit("tensor", lambda e, kc=kc, ti=ti:
                               e.matmul(ps[2][:, 0:256],
                                        ln_fm[:, kc * T + ti * 128:kc * T + ti * 128 + 128],
                                        vwb[:, kc * 256:(kc + 1) * 256],
                                        start=(kc == 0), stop=(kc == ND - 1)),
                               (ldv + P.aft("vector")) if kc == 0 else ())
                    P.emit("vector", lambda e, ti=ti, vh=vh: e.tensor_copy(
                        v_tm[:, ti * HG * 128 + vh * 256:ti * HG * 128 + vh * 256 + 256],
                        ps[2][:, 0:256]), P.aft("tensor"))

            for h4 in range(HG):
                ab = h4 % 2
                for i in range(NT):
                    nk = i // 4 + 1
                    lastw = (i % 4 + 1) * 128
                    for kc in range(nk):
                        w = 512 if kc < nk - 1 else lastw
                        P.emit("tensor", lambda e, kc=kc, i=i, h4=h4, w=w:
                               e.matmul(ps[kc][:, 0:w],
                                        q_fm[:, h4 * T + i * 128:h4 * T + i * 128 + 128],
                                        k_fm[:, h4 * T + kc * 512:h4 * T + kc * 512 + w],
                                        start=True, stop=True),
                               P.aft("scalar", "vector") if kc == 0 else ())
                    P.emit("vector", lambda e, nk=nk, lastw=lastw: e.tensor_tensor(
                        ps[nk - 1][:, lastw - 128:lastw],
                        ps[nk - 1][:, lastw - 128:lastw], mask_sb[:, :], ALU.add),
                        P.aft("tensor"))
                    mb, cmx = stat_range(8)
                    for kc in range(nk):
                        w = 512 if kc < nk - 1 else lastw
                        P.emit("vector", lambda e, kc=kc, w=w, cmx=cmx:
                               e.reduce_max(cmx[kc], ps[kc][:, 0:w], axis=AX.X),
                               P.aft("vector"))
                    mcol, nbias, cinv = cmx[4], cmx[5], cmx[7]
                    P.emit("vector", lambda e, nk=nk, mb=mb, mcol=mcol:
                           e.reduce_max(mcol, stat[:, mb:mb + nk], axis=AX.X),
                           P.aft("vector"))
                    P.emit("vector", lambda e, mcol=mcol, nbias=nbias:
                           e.tensor_scalar(nbias, mcol, -ATT_SCALE, None,
                                           ALU.mult), P.aft("vector"))
                    sb_, sums = stat_range(5)
                    for kc in range(nk):
                        w = 512 if kc < nk - 1 else lastw
                        P.emit("scalar", lambda e, kc=kc, w=w, nbias=nbias, sums=sums:
                               e.activation(p_f32[:, kc * 512:kc * 512 + w],
                                            ps[kc][:, 0:w], AF.Exp,
                                            bias=nbias, scale=ATT_SCALE,
                                            accum_out=sums[kc]),
                               P.aft("vector"))
                    P.emit("vector", lambda e, nk=nk, sb_=sb_, sums=sums:
                           e.reduce_sum(sums[4], stat[:, sb_:sb_ + nk], axis=AX.X),
                           P.aft("scalar"))
                    P.emit("vector", lambda e, sums=sums, cinv=cinv:
                           e.reciprocal(cinv, sums[4]), P.aft("vector"))
                    for kc in range(nk):
                        w = 512 if kc < nk - 1 else lastw
                        P.emit("vector", lambda e, kc=kc, w=w, cinv=cinv:
                               e.tensor_scalar(p_bf[:, kc * 512:kc * 512 + w],
                                               p_f32[:, kc * 512:kc * 512 + w],
                                               cinv, None, ALU.mult),
                               P.aft("vector"))
                    transpose_to(pt_bf, 0, p_bf[:, :], i + 1, 128)
                    for kb in range(i + 1):
                        P.emit("tensor", lambda e, kb=kb, h4=h4, i=i:
                               e.matmul(ps[4][:, 0:128],
                                        v_tm[:, kb * 512 + h4 * 128:kb * 512 + (h4 + 1) * 128],
                                        pt_bf[:, kb * 128:(kb + 1) * 128],
                                        start=(kb == 0), stop=(kb == i)),
                               P.aft("vector", "scalar") if kb == 0 else ())
                    P.emit("scalar", lambda e, i=i, ab=ab: e.copy(
                        ao_st[ab][:, i * 128:(i + 1) * 128], ps[4][:, 0:128]),
                        P.aft("tensor") + (P.aft("sync") if i == 0 else ()))
                h = g * HG + h4
                P.dma("sync", aout_blk[:, h * T:(h + 1) * T], ao_st[ab][:, :],
                      P.aft("scalar"))

    def quant_piece(dst_bf, src, s_col, inv_col):
        P.emit("vector", lambda e: e.tensor_scalar(
            f_t[:, 0:512], src, s_col, None, ALU.mult), P.aft("vector"))
        P.emit("vector", lambda e: e.tensor_copy(i32t[:, :], f_t[:, 0:512]), ())
        P.emit("vector", lambda e: e.tensor_copy(dump[:, 0:512], i32t[:, :]), ())
        P.emit("vector", lambda e: e.tensor_tensor(
            f_t[:, 0:512], f_t[:, 0:512], dump[:, 0:512], ALU.subtract), ())
        P.emit("vector", lambda e: e.tensor_scalar(
            dump[:, 512:1024], f_t[:, 0:512], -0.5, None, ALU.is_lt), ())
        P.emit("vector", lambda e: e.scalar_tensor_tensor(
            f_t[:, 512:1024], f_t[:, 0:512], 0.5, dump[:, 0:512],
            ALU.is_gt, ALU.add), ())
        P.emit("vector", lambda e: e.tensor_tensor(
            f_t[:, 0:512], f_t[:, 512:1024], dump[:, 512:1024],
            ALU.subtract), ())
        P.emit("vector", lambda e: e.tensor_scalar(
            dst_bf, f_t[:, 0:512], inv_col, None, ALU.mult), ())

    def residual_tail(i, b, x_dst, next_ln, tapname, f_from_dram=False,
                      quant_next=False):
        c_nm, c_inv = ln_stats(
            f_t[:, :], P.aft("vector", "sync") if f_from_dram else P.aft("vector"))
        P.emit("vector", lambda e, c_nm=c_nm, c_inv=c_inv: e.tensor_scalar(
            dump[:, :], f_t[:, :], c_nm, c_inv, ALU.add, ALU.mult),
            P.aft("vector", "scalar"))
        P.emit("vector", lambda e, b=b: e.tensor_tensor(
            xt[b][:, :], xt[b][:, :], dump[:, :], ALU.add),
            P.aft("sync", "vector"))
        P.dma("sync", x_dst[i * 128:(i + 1) * 128, :], xt[b][:, :],
              P.aft("vector"))
        if tapname in taps:
            P.dma("sync", taps[tapname][i * 128:(i + 1) * 128, :],
                  xt[b][:, :], P.aft("vector"))
        if next_ln or quant_next:
            c_nm2, c_inv2 = ln_stats(xt[b][:, :], P.aft("vector"))
            P.emit("vector", lambda e, b=b, c_nm2=c_nm2, c_inv2=c_inv2:
                   e.tensor_scalar(lnt[b][:, :], xt[b][:, :], c_nm2, c_inv2,
                                   ALU.add, ALU.mult),
                   P.aft("vector", "tensor"))
            if quant_next:
                _, (qm, qclip, qs, qinv, qr1) = stat_range(5)
                P.emit("vector", lambda e, b=b, qm=qm: e.reduce_max(
                    qm, lnt[b][:, :], axis=AX.X, apply_absolute_value=True),
                    P.aft("vector"))
                P.emit("vector", lambda e, qm=qm, qclip=qclip: e.tensor_scalar(
                    qclip, qm, EPS, None, ALU.max), P.aft("vector"))
                P.emit("vector", lambda e, qclip=qclip, qr1=qr1:
                       e.reciprocal(qr1, qclip), P.aft("vector"))
                P.emit("vector", lambda e, qr1=qr1, qs=qs: e.tensor_scalar(
                    qs, qr1, 127.0, None, ALU.mult), P.aft("vector"))
                P.emit("vector", lambda e, qclip=qclip, qinv=qinv:
                       e.tensor_scalar(qinv, qclip, 1.0 / 127.0, None, ALU.mult),
                       P.aft("vector"))
                for qp in range(4):
                    quant_piece(lnt[b][:, qp * 512:(qp + 1) * 512],
                                lnt[b][:, qp * 512:(qp + 1) * 512],
                                qs, qinv)
            transpose_to(ln_fm, i * 128, lnt[b][:, :], ND, T)

    def matmul_out_phase(w_src_dram, src_blk_dram, x_src, x_dst, next_ln,
                         tapname):
        P.dma("gpsimd", wo_sb[:, :], w_src_dram, P.aft("tensor", "sync"))
        ldw = P.aft("gpsimd")
        for i in range(NT):
            b = i % 2
            P.dma("gpsimd", at_blk[b][:, :],
                  src_blk_dram.rearrange("p (k t) -> p k t", t=T)
                  [:, :, i * 128:(i + 1) * 128],
                  P.aft("tensor"))
            lda = P.aft("gpsimd")
            P.dma("sync", xt[b][:, :], x_src[i * 128:(i + 1) * 128, :],
                  P.aft("vector", "scalar", "sync"))
            for er in range(4):
                for kc in range(ND):
                    P.emit("tensor", lambda e, b=b, kc=kc, er=er:
                           e.matmul(ps[er][:, :],
                                    at_blk[b][:, kc * 128:(kc + 1) * 128],
                                    wo_sb[:, kc * D + er * 512:kc * D + er * 512 + 512],
                                    start=(kc == 0), stop=(kc == ND - 1)),
                           (ldw + lda + P.aft("vector")) if kc == 0 else ())
                P.emit("vector", lambda e, er=er: e.tensor_copy(
                    f_t[:, er * 512:(er + 1) * 512], ps[er][:, :]),
                    P.aft("tensor"))
            residual_tail(i, b, x_dst, next_ln, tapname)

    def l0_ffn(x_src, x_dst, next_ln, tapname):
        NH = 8192 // 128
        P.dma("sync", b1_sb[:, :], b1_l0, P.aft("scalar"))
        P.dma("sync", b2_sb[:, :], b2bc_l0, P.aft("vector"))
        ldb = P.aft("sync")
        for hb in range(NH):
            b = hb % 2
            P.dma("gpsimd", qwb[b][:, :], w1t_blk[:, hb * D:(hb + 1) * D],
                  P.aft("tensor"))
            ldw = P.aft("gpsimd")
            for tc in range(4):
                for kc in range(ND):
                    P.emit("tensor", lambda e, b=b, kc=kc, tc=tc:
                           e.matmul(ps[tc % 2][:, :],
                                    qwb[b][:, kc * 128:(kc + 1) * 128],
                                    ln_fm[:, kc * T + tc * 512:kc * T + tc * 512 + 512],
                                    start=(kc == 0), stop=(kc == ND - 1)),
                           (ldw + ldb + P.aft("scalar")) if kc == 0 else ())
                P.emit("scalar", lambda e, tc=tc, hb=hb: e.activation(
                    h_st[:, tc * 512:(tc + 1) * 512], ps[tc % 2][:, :],
                    AF.Gelu, bias=b1_sb[:, hb:hb + 1], scale=1.0),
                    P.aft("tensor") + (P.aft("sync") if tc == 0 else ()))
            P.dma("sync", h_blk[:, hb * T:(hb + 1) * T], h_st[:, :],
                  P.aft("scalar"))
        if "h0" in taps:
            for hb in range(16):
                P.dma("gpsimd", f_t[:, :], h_blk[:, hb * T:(hb + 1) * T],
                      P.aft("sync"))
                P.dma("sync", taps["h0"][:, hb * T:(hb + 1) * T], f_t[:, :],
                      P.aft("gpsimd"))
        for er in range(4):
            P.dma("gpsimd", w2blk[:, :],
                  w2t_b512[:, er * (64 * 512):(er + 1) * (64 * 512)],
                  P.aft("tensor"))
            ldw2 = P.aft("gpsimd")
            for i in range(NT):
                for kh in range(2):
                    P.dma("gpsimd", at_ffn[:, :],
                          h_blk.rearrange("p (k t) -> p k t", t=T)
                          [:, kh * 32:(kh + 1) * 32, i * 128:(i + 1) * 128],
                          P.aft("tensor"))
                    lda = P.aft("gpsimd")
                    for kc8 in range(32):
                        kc = kh * 32 + kc8
                        P.emit("tensor", lambda e, kc=kc, kc8=kc8:
                               e.matmul(ps[0][:, :],
                                        at_ffn[:, kc8 * 128:(kc8 + 1) * 128],
                                        w2blk[:, kc * 512:(kc + 1) * 512],
                                        start=(kc == 0), stop=(kc == 63)),
                               (ldw2 + lda + P.aft("vector")) if kc8 == 0 else ())
                P.emit("vector", lambda e, er=er: e.tensor_tensor(
                    f_t[:, 0:512], ps[0][:, :], b2_sb[:, er * 512:(er + 1) * 512],
                    ALU.add), P.aft("tensor", "sync"))
                P.dma("sync", f_dram[i * 128:(i + 1) * 128, er * 512:(er + 1) * 512],
                      f_t[:, 0:512], P.aft("vector"))
        for i in range(NT):
            b = i % 2
            P.dma("sync", xt[b][:, :], x_src[i * 128:(i + 1) * 128, :],
                  P.aft("vector", "scalar", "sync"))
            P.dma("sync", f_t[:, :], f_dram[i * 128:(i + 1) * 128, :],
                  P.aft("vector", "scalar", "sync"))
            if "ffn0" in taps and tapname == "x2":
                P.dma("sync", taps["ffn0"][i * 128:(i + 1) * 128, :], f_t[:, :],
                      P.aft("sync"))
            residual_tail(i, b, x_dst, next_ln, tapname, f_from_dram=True)

    def l1_attn(x_src, x_dst, quant_next, tapname):
        CH = 256
        NCk = T // CH
        mcol = mcol_area
        dtb_col, alog_col, a_col = mcol[:, 0:1], mcol[:, 1:2], mcol[:, 2:3]
        P.dma("sync", cw_sb, cw_in, P.aft("vector", "scalar"))
        P.dma("sync", cb_sb[:, 0:34], cb_in, P.aft("vector"))
        P.dma("sync", dbc_sb, dbc_in, P.aft("vector"))
        P.dma("sync", onesf[:, :], onesf_in, P.aft("vector"))
        P.dma("sync", idf_sb[:, :], identf_in, P.aft("vector"))
        P.dma("sync", mcol[:, 0:1], dtb_in, P.aft("vector"))
        P.dma("sync", mcol[:, 1:2], alog_in, P.aft("vector"))
        prm = P.aft("sync")

        for ob in range(67):
            b = ob % 2
            P.dma("gpsimd", qwb[b][:, :], wip_blk[:, ob * D:(ob + 1) * D],
                  P.aft("tensor"))
            ldw = P.aft("gpsimd")
            for tc in range(4):
                for kc in range(ND):
                    P.emit("tensor", lambda e, b=b, kc=kc, tc=tc:
                           e.matmul(ps[tc % 2][:, :],
                                    qwb[b][:, kc * 128:(kc + 1) * 128],
                                    ln_fm[:, kc * T + tc * 512:kc * T + tc * 512 + 512],
                                    start=(kc == 0), stop=(kc == ND - 1)),
                           (ldw + P.aft("scalar", "vector")) if kc == 0 else ())
                if ob < 32:
                    P.emit("scalar", lambda e, tc=tc: e.copy(
                        g1_st[:, tc * 512:(tc + 1) * 512], ps[tc % 2][:, :]),
                        P.aft("tensor") + (P.aft("sync") if tc == 0 else ()))
                elif ob < 66:
                    P.emit("scalar", lambda e, tc=tc: e.copy(
                        x1_st[:, tc * 512:(tc + 1) * 512], ps[tc % 2][:, :]),
                        P.aft("tensor") + (P.aft("sync") if tc == 0 else ()))
                else:
                    P.emit("vector", lambda e, tc=tc: e.tensor_copy(
                        f_t[:, tc * 512:(tc + 1) * 512], ps[tc % 2][:, :]),
                        P.aft("tensor"))
            if ob < 32:
                P.dma("sync", gate_blk[:, ob * T:(ob + 1) * T], g1_st[:, :],
                      P.aft("scalar"))
            elif ob < 66:
                P.dma("sync", xbc_blk[:, (ob - 32) * T:(ob - 31) * T], x1_st[:, :],
                      P.aft("scalar"))

        P.dma("sync", masklo_sb, masklo_in, P.aft("tensor"))
        P.emit("scalar", lambda e: e.activation(a_col, alog_col, AF.Exp,
                                                bias=0.0, scale=1.0),
               prm + P.aft("vector"))
        P.emit("vector", lambda e: e.tensor_scalar(a_col, a_col, -1.0, None,
                                                   ALU.mult), P.aft("scalar"))
        P.emit("scalar", lambda e: e.activation(dump[0:64, :], f_t[0:64, :],
                                                AF.Identity, bias=dtb_col[0:64, :],
                                                scale=1.0),
               P.aft("vector"))
        P.emit("scalar", lambda e: e.activation(f_t[0:64, :], dump[0:64, :],
                                                AF.Abs, bias=0.0, scale=1.0),
               P.aft("scalar"))
        P.emit("scalar", lambda e: e.activation(f_t[0:64, :], f_t[0:64, :],
                                                AF.Exp, bias=0.0, scale=-1.0),
               P.aft("scalar"))
        P.emit("scalar", lambda e: e.activation(f_t[0:64, :], f_t[0:64, :],
                                                AF.Ln, bias=1.0, scale=1.0),
               P.aft("scalar"))
        P.emit("vector", lambda e: e.tensor_scalar(dump[0:64, :], dump[0:64, :],
                                                   0.0, None, ALU.max),
               P.aft("scalar"))
        P.emit("vector", lambda e: e.tensor_tensor(dump[0:64, :], dump[0:64, :],
                                                   f_t[0:64, :], ALU.add),
               P.aft("vector"))
        P.emit("vector", lambda e: e.tensor_scalar(f_t[0:64, :], dump[0:64, :],
                                                   a_col[0:64, :], None, ALU.mult),
               P.aft("vector"))
        cum_c = cumdec_c[0:64, 0:256]
        dec_c = cumdec_c[0:64, 256:512]
        P.emit("vector", lambda e: e.memset(psb, 0.0), P.aft("vector", "scalar"))
        P.emit("vector", lambda e: e.memset(pbf_sb, 0.0), ())
        for fb in range(34):
            P.emit("vector", lambda e, fb=fb: e.memset(
                xraw_sb[:, fb * 260:fb * 260 + 3], 0.0), ())
        for c in range(NCk):
            t1_eT = dec_c
            P.emit("vector", lambda e: e.memset(ca_sb, 1.0),
                   P.aft("vector", "tensor", "scalar"))
            P.emit("vector", lambda e, c=c: e.tensor_tensor_scan(
                cum_c, ca_sb[0:64, :], f_t[0:64, c * CH:(c + 1) * CH],
                c_all[0:64, 0:1], ALU.mult, ALU.add),
                P.aft("vector", "tensor", "scalar"))
            P.emit("vector", lambda e: e.tensor_scalar(
                t1_eT[0:64, 0:128], onesf[0:64, 0:128], cum_c[:, 255:256], None,
                ALU.mult), P.aft("vector"))
            P.emit("tensor", lambda e: e.transpose(
                ps[2][:, 384:448], t1_eT[0:64, 0:128], idf_sb[0:64, 0:64]),
                P.aft("vector"))
            P.emit("scalar", lambda e, c=c: e.activation(
                eT_all[:, 0:64], ps[2][:, 384:448], AF.Exp,
                bias=0.0, scale=1.0), P.aft("tensor"))
            for j in range(2):
                P.emit("tensor", lambda e, j=j: e.transpose(
                    ps[2][:, 0:64], cum_c[:, j * 128:(j + 1) * 128],
                    idf_sb[0:64, 0:64]), P.aft("vector", "scalar"))
                P.emit("scalar", lambda e, c=c, j=j: e.activation(
                    ncum_tm[:, j * 64:(j + 1) * 64],
                    ps[2][:, 0:64], AF.Identity, bias=0.0, scale=-1.0),
                    P.aft("tensor"))
                P.emit("scalar", lambda e, c=c, j=j: e.activation(
                    ec_tm[:, j * 64:(j + 1) * 64],
                    ps[2][:, 0:64], AF.Exp, bias=0.0, scale=1.0), ())
                P.emit("tensor", lambda e, c=c, j=j: e.transpose(
                    ps[2][:, 64:128],
                    dump[0:64, c * CH + j * 128:c * CH + (j + 1) * 128],
                    idf_sb[0:64, 0:64]), P.aft("scalar"))
                P.emit("vector", lambda e, c=c, j=j: e.tensor_copy(
                    dt_tm[:, j * 64:(j + 1) * 64],
                    ps[2][:, 64:128]), P.aft("tensor"))
            P.emit("scalar", lambda e: e.activation(
                dec_c, cum_c, AF.Exp, bias=cum_c[:, 255:256], scale=-1.0),
                P.aft("vector"))
            for j in range(2):
                P.emit("tensor", lambda e, j=j: e.transpose(
                    ps[2][:, 128:192], dec_c[:, j * 128:(j + 1) * 128],
                    idf_sb[0:64, 0:64]), P.aft("vector", "scalar"))
                P.emit("vector", lambda e, c=c, j=j: e.tensor_copy(
                    dec_tm[:, j * 64:(j + 1) * 64],
                    ps[2][:, 128:192]), P.aft("tensor"))

            for fb in range(34):
                base = fb * 260
                if c > 0:
                    P.emit("vector", lambda e, base=base: e.tensor_copy(
                        xraw_sb[:, base:base + 3],
                        xraw_sb[:, base + 256:base + 259]),
                        P.aft("scalar") if fb == 0 else ())
                P.dma("sync", xraw_sb[:, base + 3:base + 259],
                      xbc_blk[:, fb * T + c * CH:fb * T + (c + 1) * CH],
                      P.aft("vector"))
                lx = P.aft("sync")
                P.emit("vector", lambda e, base=base, fb=fb: e.tensor_scalar(
                    ca_sb, xraw_sb[:, base:base + 256],
                    cw_sb[:, fb * 4:fb * 4 + 1], cb_sb[:, fb:fb + 1],
                    ALU.mult, ALU.add), lx)
                for i_ in range(1, 4):
                    P.emit("vector", lambda e, base=base, fb=fb, i_=i_:
                           e.scalar_tensor_tensor(
                               ca_sb, xraw_sb[:, base + i_:base + i_ + 256],
                               cw_sb[:, fb * 4 + i_:fb * 4 + i_ + 1], ca_sb,
                               ALU.mult, ALU.add), ())
                if fb < 32:
                    P.emit("scalar", lambda e: e.activation(
                        mt_sb[:, 0:256], ca_sb, AF.Silu, bias=0.0, scale=1.0),
                        P.aft("vector", "tensor"))
                    for j in range(2):
                        P.emit("tensor", lambda e, j=j: e.transpose(
                            pt[j][:, 0:128], mt_sb[:, j * 128:(j + 1) * 128],
                            id_bf[:, :]), P.aft("scalar", "vector"))
                        P.emit("vector", lambda e, j=j, fb=fb: e.tensor_copy(
                            xtm_sb[:, j * 4096 + fb * 128:j * 4096 + (fb + 1) * 128],
                            pt[j][:, 0:128]), P.aft("tensor"))
                elif fb == 32:
                    P.emit("scalar", lambda e: e.activation(
                        bf_sb, ca_sb, AF.Silu, bias=0.0, scale=1.0),
                        P.aft("vector", "tensor"))
                    for j in range(2):
                        P.emit("tensor", lambda e, j=j: e.transpose(
                            pt[j][:, 0:128], bf_sb[:, j * 128:(j + 1) * 128],
                            id_bf[:, :]), P.aft("scalar", "vector"))
                        P.emit("vector", lambda e, j=j: e.tensor_copy(
                            btm_sb[:, j * 128:(j + 1) * 128], pt[j][:, 0:128]),
                            P.aft("tensor"))
                else:
                    P.emit("scalar", lambda e: e.activation(
                        cf_sb, ca_sb, AF.Silu, bias=0.0, scale=1.0),
                        P.aft("vector", "tensor"))
            for st in range(2):
                P.emit("tensor", lambda e, st=st: e.matmul(
                    ps[st][:, 0:256], bf_sb[:, st * 128:(st + 1) * 128],
                    cf_sb[:, 0:256], start=True, stop=True),
                    P.aft("vector", "scalar") if st == 0 else ())
            for hg in range(8):
                h0 = hg * 8
                for h in range(h0, h0 + 8):
                    for st in range(2):
                        P.emit("vector", lambda e, h=h, h0=h0, st=st, c=c:
                               e.tensor_scalar(
                                   xdt8_sb[:, st * 512 + (h - h0) * 64:
                                           st * 512 + (h - h0 + 1) * 64],
                                   xtm_sb[:, st * 4096 + h * 64:st * 4096 + (h + 1) * 64],
                                   dt_tm[:, st * 64 + h:st * 64 + h + 1],
                                   None, ALU.mult),
                               P.aft("tensor") if (h == h0 and st == 0) else ())
                        P.emit("vector", lambda e, h=h, h0=h0, st=st, c=c:
                               e.tensor_scalar(
                                   xdd8_sb[:, st * 512 + (h - h0) * 64:
                                           st * 512 + (h - h0 + 1) * 64],
                                   xdt8_sb[:, st * 512 + (h - h0) * 64:
                                           st * 512 + (h - h0 + 1) * 64],
                                   dec_tm[:, st * 64 + h:st * 64 + h + 1],
                                   None, ALU.mult), ())
                for j in range(2):
                    P.emit("tensor", lambda e, j=j, h0=h0: e.matmul(
                        ps[4][:, 0:512], cf_sb[:, j * 128:(j + 1) * 128],
                        pbf_sb[:, h0 * 64:(h0 + 8) * 64], start=True, stop=True),
                        P.aft("scalar", "vector"))
                    for h in range(h0, h0 + 8):
                        for st in range(j + 1):
                            k = st * 2 + j - st
                            P.emit("vector", lambda e, c=c, j=j, h=h:
                                   e.tensor_scalar(
                                       ca_sb[:, 0:128], onesf[:, 0:128],
                                       ncum_tm[:, j * 64 + h:j * 64 + h + 1],
                                       None, ALU.mult), P.aft("tensor", "scalar"))
                            P.emit("tensor", lambda e: e.transpose(
                                ps[2][:, 0:128], ca_sb[:, 0:128], idf_sb[:, :]),
                                P.aft("vector", "scalar"))
                            if st == j:
                                P.emit("vector", lambda e: e.scalar_tensor_tensor(
                                    lt_sb[:, 0:128], masklo_sb[:, :], -1.0,
                                    ps[2][:, 0:128], ALU.mult, ALU.add),
                                    P.aft("tensor"))
                                P.emit("scalar", lambda e, c=c, st=st, h=h:
                                       e.activation(
                                           lt_sb[:, 0:128], lt_sb[:, 0:128],
                                           AF.Exp,
                                           bias=ncum_tm[:, st * 64 + h:st * 64 + h + 1],
                                           scale=-1.0), P.aft("vector"))
                            else:
                                P.emit("scalar", lambda e, c=c, st=st, h=h:
                                       e.activation(
                                           lt_sb[:, 0:128], ps[2][:, 0:128],
                                           AF.Exp,
                                           bias=ncum_tm[:, st * 64 + h:st * 64 + h + 1],
                                           scale=-1.0), P.aft("tensor"))
                            P.emit("vector", lambda e, st=st, j=j: e.tensor_tensor(
                                mt_sb[:, 256:384], ps[st][:, j * 128:(j + 1) * 128],
                                lt_sb[:, 0:128], ALU.mult), P.aft("scalar"))
                            P.emit("tensor", lambda e, st=st, j=j, h=h, h0=h0:
                                   e.matmul(
                                       ps[3][:, (h - h0) * 64:(h - h0 + 1) * 64],
                                       mt_sb[:, 256:384],
                                       xdt8_sb[:, st * 512 + (h - h0) * 64:
                                               st * 512 + (h - h0 + 1) * 64],
                                       start=(st == 0), stop=(st == j)),
                                   P.aft("vector"))
                    for h in range(h0, h0 + 8):
                        P.emit("scalar", lambda e, c=c, j=j, h=h, h0=h0:
                               e.activation(
                                   yo_sb[:, 0:64],
                                   ps[4][:, (h - h0) * 64:(h - h0 + 1) * 64],
                                   AF.Identity, bias=0.0,
                                   scale=ec_scale(c, j, h)), P.aft("tensor"))
                        P.emit("vector", lambda e, j=j, h=h, h0=h0:
                               e.scalar_tensor_tensor(
                                   yo_sb[:, 64:128],
                                   xtm_sb[:, j * 4096 + h * 64:j * 4096 + (h + 1) * 64],
                                   dbc_sb[:, h:h + 1],
                                   ps[3][:, (h - h0) * 64:(h - h0 + 1) * 64],
                                   ALU.mult, ALU.add), P.aft("scalar"))
                        P.emit("vector", lambda e, j=j, h=h: e.tensor_tensor(
                            ytm_sb[:, j * 4096 + h * 64:j * 4096 + (h + 1) * 64],
                            yo_sb[:, 64:128], yo_sb[:, 0:64], ALU.add),
                            P.aft("vector") + (P.aft("sync") if (hg == 0 and h == h0) else ()))
                for st in range(2):
                    P.emit("tensor", lambda e, st=st: e.matmul(
                        ps[5][:, 0:512], btm_sb[:, st * 128:(st + 1) * 128],
                        xdd8_sb[:, st * 512:(st + 1) * 512],
                        start=(st == 0), stop=(st == 1)),
                        P.aft("vector", "scalar") if st == 0 else ())
                for h in range(h0, h0 + 8):
                    P.emit("vector", lambda e, c=c, h=h, h0=h0:
                           e.scalar_tensor_tensor(
                               psb[:, h * 64:(h + 1) * 64],
                               psb[:, h * 64:(h + 1) * 64],
                               eT_all[:, h:h + 1],
                               ps[5][:, (h - h0) * 64:(h - h0 + 1) * 64],
                               ALU.mult, ALU.add),
                           P.aft("tensor") if h == h0 else ())
                    P.emit("vector", lambda e, h=h: e.tensor_copy(
                        pbf_sb[:, h * 64:(h + 1) * 64],
                        psb[:, h * 64:(h + 1) * 64]), ())
            for j in range(2):
                P.dma("sync", y_blk[:, (2 * c + j) * 4096:(2 * c + j + 1) * 4096],
                      ytm_sb[:, j * 4096:(j + 1) * 4096], P.aft("vector"))

        if "l1dbg" in taps:
            tp = taps["l1dbg"]
            dw = P.aft("vector", "scalar", "tensor")
            P.dma("sync", tp[:, 0:2048], dump[:, :], dw)
            P.dma("sync", tp[:, 2048:4096], f_t[:, :], dw)
            P.dma("sync", tp[:, 4096:4224], ncum_tm, dw)
            P.dma("sync", tp[:, 4224:4352], dt_tm, dw)
            P.dma("sync", tp[:, 4352:4480], dec_tm, dw)
            P.dma("sync", tp[:, 4480:4608], ec_tm, dw)
            P.dma("sync", tp[:, 4608:4672], eT_all, dw)
            P.dma("sync", tp[:, 4672:8768], psb, dw)
            P.dma("sync", tp[:, 8768:8832], dbc_sb, dw)
            P.dma("gpsimd", tp[:, 8832:9088], bf_sb, dw)
            P.dma("gpsimd", tp[:, 9088:9344], cf_sb, dw)
        P.dma("sync", nw_sb, nw_in, P.aft("vector", "scalar", "sync"))
        for i in range(NT):
            P.dma("sync", ya_sb, y_blk[:, i * 4096:(i + 1) * 4096],
                  P.aft("vector", "scalar", "tensor"))
            P.dma("gpsimd", gb_sb,
                  gate_blk.rearrange("p (k t) -> p k t", t=T)
                  [:, :, i * 128:(i + 1) * 128], P.aft("tensor"))
            lg = P.aft("sync", "gpsimd")
            for m in range(32):
                P.emit("tensor", lambda e, m=m: e.transpose(
                    pt[m % 2][:, 0:128], gb_sb[:, m * 128:(m + 1) * 128],
                    id_bf[:, :]), lg + P.aft("vector") if m < 2 else P.aft("vector"))
                P.emit("vector", lambda e, m=m: e.tensor_copy(
                    gtm_sb[:, m * 128:(m + 1) * 128], pt[m % 2][:, 0:128]),
                    P.aft("tensor"))
            P.emit("scalar", lambda e: e.activation(
                sg_sb, gtm_sb, AF.Silu, bias=0.0, scale=1.0), P.aft("vector"))
            P.emit("vector", lambda e: e.tensor_tensor(
                sg_sb, sg_sb, ya_sb, ALU.mult), P.aft("scalar", "sync"))
            _, (qc, qstd, qinv, _u) = stat_range(4)
            P.emit("scalar", lambda e, qc=qc: e.activation(
                hx_sb, sg_sb, AF.Square, bias=0.0, scale=1.0, accum_out=qc),
                P.aft("vector"))
            P.emit("scalar", lambda e, qc=qc, qstd=qstd: e.activation(
                qstd, qc, AF.Sqrt, bias=EPS, scale=1.0 / 4096.0),
                P.aft("scalar"))
            P.emit("vector", lambda e, qstd=qstd, qinv=qinv:
                   e.reciprocal(qinv, qstd), P.aft("scalar"))
            P.emit("vector", lambda e, qinv=qinv: e.scalar_tensor_tensor(
                hx_sb, sg_sb, qinv, nw_sb, ALU.mult, ALU.mult),
                P.aft("vector", "scalar"))
            for m in range(32):
                P.emit("tensor", lambda e, m=m: e.transpose(
                    pt[m % 2][:, 0:128], hx_sb[:, m * 128:(m + 1) * 128],
                    id_bf[:, :]), P.aft("vector"))
                P.emit("vector", lambda e, m=m: e.tensor_copy(
                    gb_sb[:, m * 128:(m + 1) * 128], pt[m % 2][:, 0:128]),
                    P.aft("tensor"))
            P.dma("sync", hgn_blk.rearrange("p (k t) -> p k t", t=T)
                  [:, :, i * 128:(i + 1) * 128], gb_sb, P.aft("vector"))

        for er in range(4):
            P.dma("gpsimd", arena[:, 0:16384],
                  wot_b512[:, er * 16384:(er + 1) * 16384],
                  P.aft("tensor", "sync"))
            ldw2 = P.aft("gpsimd")
            for i in range(NT):
                P.dma("gpsimd", at_ffn[:, 0:4096],
                      hgn_blk.rearrange("p (k t) -> p k t", t=T)
                      [:, :, i * 128:(i + 1) * 128], P.aft("tensor"))
                lda = P.aft("gpsimd")
                for kc in range(32):
                    P.emit("tensor", lambda e, kc=kc: e.matmul(
                        ps[0][:, :], at_ffn[:, kc * 128:(kc + 1) * 128],
                        arena[:, kc * 512:(kc + 1) * 512],
                        start=(kc == 0), stop=(kc == 31)),
                        (ldw2 + lda + P.aft("vector")) if kc == 0 else ())
                P.emit("vector", lambda e: e.tensor_copy(
                    f_t[:, 0:512], ps[0][:, :]), P.aft("tensor", "sync"))
                P.dma("sync", f_dram[i * 128:(i + 1) * 128, er * 512:(er + 1) * 512],
                      f_t[:, 0:512], P.aft("vector"))
        if "l1gate" in taps:
            P.dma("sync", taps["l1gate"], gate_blk[:, :], P.aft("sync"))
        if "l1xbc" in taps:
            P.dma("sync", taps["l1xbc"], xbc_blk[:, :], P.aft("sync"))
        if "l1y" in taps:
            P.dma("sync", taps["l1y"], y_blk[:, :], P.aft("sync"))
        for i in range(NT):
            b = i % 2
            P.dma("sync", xt[b][:, :], x_src[i * 128:(i + 1) * 128, :],
                  P.aft("vector", "scalar", "sync"))
            P.dma("sync", f_t[:, :], f_dram[i * 128:(i + 1) * 128, :],
                  P.aft("vector", "scalar", "sync"))
            residual_tail(i, b, x_dst, next_ln=False, tapname=tapname,
                          f_from_dram=True, quant_next=quant_next)

    def ec_scale(c, j, h):
        return ec_tm[:, j * 64 + h:j * 64 + h + 1]

    def l1_ffn(x_src, x_dst, tapname):
        NW1 = 8192 * 2048
        _, (acc1, acc2, tmpc) = stat_range(3)
        wthr = mcol_area[:, 8:14]
        mcol32 = mcol_area[:, 14:15]
        for wi, (wsrc, ntile) in enumerate(((w1t_l1_b512, 64), (w2t_l1_b512, 64))):
            acc = acc1 if wi == 0 else acc2
            P.emit("vector", lambda e, acc=acc: e.memset(acc, 0.0),
                   P.aft("scalar"))
            for jt in range(ntile):
                b = jt % 2
                P.dma("sync", xt[b][:, :], wsrc[:, jt * 2048:(jt + 1) * 2048],
                      P.aft("vector"))
                P.emit("vector", lambda e, b=b, tmpc=tmpc: e.reduce_sum(
                    tmpc, xt[b][:, :], axis=AX.X, apply_absolute_value=True),
                    P.aft("sync"))
                P.emit("vector", lambda e, acc=acc, tmpc=tmpc: e.tensor_tensor(
                    acc, acc, tmpc, ALU.add), P.aft("vector"))
            P.emit("tensor", lambda e, acc=acc: e.matmul(
                ps[2][0:1, 0:1], onesf[:, 0:1], acc, start=True, stop=True),
                P.aft("vector"))
            P.emit("vector", lambda e: e.tensor_copy(mcol32[0:1, 0:1],
                                                     ps[2][0:1, 0:1]),
                   P.aft("tensor"))
            P.emit("tensor", lambda e: e.matmul(
                ps[2][:, 4:5], onesf[0:1, 0:128], mcol32[0:1, 0:1],
                start=True, stop=True), P.aft("vector"))
            base = wi * 3
            P.emit("vector", lambda e, base=base: e.tensor_scalar(
                wthr[:, base + 2:base + 3], ps[2][:, 4:5], 1.0 / NW1, EPS,
                ALU.mult, ALU.max), P.aft("tensor"))
            P.emit("vector", lambda e, base=base: e.tensor_scalar(
                wthr[:, base:base + 1], wthr[:, base + 2:base + 3], 0.5, None,
                ALU.mult), P.aft("vector"))
            P.emit("vector", lambda e, base=base: e.tensor_scalar(
                wthr[:, base + 1:base + 2], wthr[:, base:base + 1], -1.0, None,
                ALU.mult), P.aft("vector"))

        def quant_w_tile(dst_bf, src_f32, base, waits):
            P.emit("vector", lambda e: e.tensor_scalar(
                dump[:, 0:512], src_f32, wthr[:, base:base + 1], None,
                ALU.is_gt), waits)
            P.emit("vector", lambda e: e.tensor_scalar(
                dump[:, 512:1024], src_f32, wthr[:, base + 1:base + 2], None,
                ALU.is_lt), ())
            P.emit("vector", lambda e: e.tensor_tensor(
                dst_bf, dump[:, 0:512], dump[:, 512:1024], ALU.subtract), ())

        for g in range(4):
            for hr in range(16):
                for kc in range(ND):
                    wslot = wbuf[:, (kc % 4) * 512:(kc % 4) * 512 + 512]
                    if g == 0:
                        b = kc % 2
                        P.dma("sync", xt[b][:, 0:512],
                              w1t_l1_b512[:, hr * 8192 + kc * 512:hr * 8192 + (kc + 1) * 512],
                              P.aft("vector", "scalar"))
                        quant_w_tile(wslot, xt[b][:, 0:512], 0,
                                     P.aft("sync", "tensor"))
                        P.dma("sync", wq1_scr[:, hr * 8192 + kc * 512:
                                              hr * 8192 + (kc + 1) * 512],
                              wslot, P.aft("vector"))
                    else:
                        P.dma("sync", wslot,
                              wq1_scr[:, hr * 8192 + kc * 512:hr * 8192 + (kc + 1) * 512],
                              P.aft("tensor"))
                    lw = P.aft("sync", "vector")
                    for tt_ in range(4):
                        i = g * 4 + tt_
                        P.emit("tensor", lambda e, tt_=tt_, kc=kc, i=i, wslot=wslot:
                               e.matmul(ps[tt_][:, :],
                                        ln_fm[:, kc * T + i * 128:kc * T + i * 128 + 128],
                                        wslot, start=(kc == 0), stop=(kc == ND - 1)),
                               (lw + P.aft("scalar")) if kc == 0 else lw)
                for tt_ in range(4):
                    P.emit("scalar", lambda e, tt_=tt_, hr=hr: e.activation(
                        arena[:, tt_ * 8192 + hr * 512:tt_ * 8192 + (hr + 1) * 512],
                        ps[tt_][:, :], AF.Relu, bias=0.0,
                        scale=wthr[:, 2:3]), P.aft("tensor"))
            for tt_ in range(4):
                i = g * 4 + tt_
                hsl = arena[:, tt_ * 8192:(tt_ + 1) * 8192]
                _, (hm, hclip, hs, hinv, hr1) = stat_range(5)
                P.emit("vector", lambda e, hsl=hsl, hm=hm: e.reduce_max(
                    hm, hsl, axis=AX.X, apply_absolute_value=True),
                    P.aft("scalar"))
                P.emit("vector", lambda e, hm=hm, hclip=hclip: e.tensor_scalar(
                    hclip, hm, EPS, None, ALU.max), P.aft("vector"))
                P.emit("vector", lambda e, hclip=hclip, hr1=hr1:
                       e.reciprocal(hr1, hclip), P.aft("vector"))
                P.emit("vector", lambda e, hr1=hr1, hs=hs: e.tensor_scalar(
                    hs, hr1, 127.0, None, ALU.mult), P.aft("vector"))
                P.emit("vector", lambda e, hclip=hclip, hinv=hinv:
                       e.tensor_scalar(hinv, hclip, 1.0 / 127.0, None, ALU.mult),
                       P.aft("vector"))
                for pc in range(16):
                    sl = hsl[:, pc * 512:(pc + 1) * 512]
                    quant_piece(sl, sl, hs, hinv)
                for m in range(64):
                    P.emit("tensor", lambda e, m=m, hsl=hsl: e.transpose(
                        pt[m % 2][:, 0:128], hsl[:, m * 128:(m + 1) * 128],
                        id_bf[:, :]), P.aft("vector"))
                    P.emit("vector", lambda e, m=m, hsl=hsl: e.tensor_copy(
                        hsl[:, m * 128:(m + 1) * 128], pt[m % 2][:, 0:128]),
                        P.aft("tensor"))
                P.dma("sync", h_blk.rearrange("p (k t) -> p k t", t=T)
                      [:, :, i * 128:(i + 1) * 128], hsl, P.aft("vector"))
        for er in range(4):
            for kc in range(64):
                b = kc % 2
                P.dma("sync", xt[b][:, 0:512],
                      w2t_l1_b512[:, er * 32768 + kc * 512:er * 32768 + (kc + 1) * 512],
                      P.aft("vector", "scalar"))
                quant_w_tile(arena[:, kc * 512:(kc + 1) * 512], xt[b][:, 0:512],
                             3, P.aft("sync", "tensor"))
            for i in range(NT):
                for kh in range(2):
                    P.dma("gpsimd", at_ffn[:, :],
                          h_blk.rearrange("p (k t) -> p k t", t=T)
                          [:, kh * 32:(kh + 1) * 32, i * 128:(i + 1) * 128],
                          P.aft("tensor"))
                    lda = P.aft("gpsimd")
                    for kc8 in range(32):
                        kc = kh * 32 + kc8
                        P.emit("tensor", lambda e, kc=kc, kc8=kc8: e.matmul(
                            ps[0][:, :], at_ffn[:, kc8 * 128:(kc8 + 1) * 128],
                            arena[:, kc * 512:(kc + 1) * 512],
                            start=(kc == 0), stop=(kc == 63)),
                            (lda + P.aft("scalar", "vector")) if kc8 == 0 else ())
                P.emit("scalar", lambda e: e.activation(
                    f_t[:, 0:512], ps[0][:, :], AF.Identity, bias=0.0,
                    scale=wthr[:, 5:6]), P.aft("tensor", "sync"))
                P.dma("sync", f_dram[i * 128:(i + 1) * 128, er * 512:(er + 1) * 512],
                      f_t[:, 0:512], P.aft("scalar"))
        for i in range(NT):
            b = i % 2
            P.dma("sync", xt[b][:, :], x_src[i * 128:(i + 1) * 128, :],
                  P.aft("vector", "scalar", "sync"))
            P.dma("sync", f_t[:, :], f_dram[i * 128:(i + 1) * 128, :],
                  P.aft("vector", "scalar", "sync"))
            residual_tail(i, b, x_dst, next_ln=False, tapname=tapname,
                          f_from_dram=True)

    if N_SUBLAYERS >= 1:
        l0_attn()
        matmul_out_phase(wo_blk, aout_blk, x_in, xres[0],
                         next_ln=(N_SUBLAYERS >= 2), tapname="x1")
    if N_SUBLAYERS >= 2:
        l0_ffn(xres[0], xres[1], next_ln=(N_SUBLAYERS >= 3), tapname="x2")
    if N_SUBLAYERS >= 3:
        l1_attn(xres[1], xres[0], quant_next=(N_SUBLAYERS >= 4), tapname="x3")
    if N_SUBLAYERS >= 4:
        l1_ffn(xres[0], out, tapname="")

    x_cur = [x_in, xres[0], xres[1], xres[0]][min(N_SUBLAYERS, 3)]
    if N_SUBLAYERS < 4:
        for i in range(NT):
            b = i % 2
            P.dma("sync", xt[b][:, :], x_cur[i * 128:(i + 1) * 128, :],
                  P.aft("sync", "vector", "scalar"))
            P.dma("sync", out[i * 128:(i + 1) * 128, :], xt[b][:, :],
                  P.aft("sync"))

    P.emit("sync", lambda e: e.nop(), P.aft("sync", "gpsimd"))
    P.finalize()
    return nc


_NC_CACHE = {}


def _get_nc():
    key = (N_SUBLAYERS, tuple(DEBUG_TAPS))
    if key not in _NC_CACHE:
        _NC_CACHE[key] = build_nc()
    return _NC_CACHE[key]


def _prep_weights(inputs):
    f32c = np.ascontiguousarray

    def blk_lhst(w):
        K, O = w.shape
        return f32c(w.reshape(K // 128, 128, O // 128, 128)
                    .transpose(1, 2, 0, 3).reshape(128, O * K // 128))

    def blk_rhs(w, nb):
        K, O = w.shape
        return f32c(w.reshape(K // 128, 128, O // nb, nb)
                    .transpose(1, 2, 0, 3).reshape(128, O * K // 128))

    w = {}
    wqkv = np.asarray(inputs["L0_attn_w_qkv"], np.float32)
    w["wqkv_blk"] = blk_lhst(wqkv)
    w["wqkv_v"] = blk_rhs(wqkv, 256)
    w["wo_blk"] = blk_rhs(np.asarray(inputs["L0_attn_w_out"], np.float32), 2048)
    w["w1t_blk"] = blk_lhst(np.asarray(inputs["L0_ffn_proj1"], np.float32).T)
    w["w2t_b512"] = blk_rhs(np.asarray(inputs["L0_ffn_proj2"], np.float32).T, 512)
    w["b1_l0"] = f32c(np.asarray(inputs["L0_ffn_proj1_bias"], np.float32)
                      .reshape(64, 128).T)
    w["b2bc_l0"] = f32c(np.broadcast_to(
        np.asarray(inputs["L0_ffn_proj2_bias"], np.float32), (128, D)))
    ipw = np.asarray(inputs["L1_attn_in_proj_w"], np.float32)
    ipt = np.zeros((D, 8576), np.float32)
    ipt[:, :8512] = ipw.T
    w["wip_blk"] = blk_lhst(ipt)
    w["wot_b512"] = blk_rhs(np.asarray(inputs["L1_attn_out_proj_w"], np.float32).T, 512)
    cw = np.asarray(inputs["L1_attn_conv_w"], np.float32)
    w["cw_in"] = f32c(cw.reshape(34, 128, 4).transpose(1, 0, 2).reshape(128, 136))
    cb = np.asarray(inputs["L1_attn_conv_b"], np.float32)
    w["cb_in"] = f32c(cb.reshape(34, 128).T)
    dtb = np.zeros((128, 1), np.float32)
    dtb[:64, 0] = np.asarray(inputs["L1_attn_dt_bias"], np.float32)
    w["dtb_in"] = dtb
    alog = np.zeros((128, 1), np.float32)
    alog[:64, 0] = np.asarray(inputs["L1_attn_A_log"], np.float32)
    w["alog_in"] = alog
    w["dbc_in"] = f32c(np.broadcast_to(
        np.asarray(inputs["L1_attn_D"], np.float32), (128, 64)))
    w["nw_in"] = f32c(np.broadcast_to(
        np.asarray(inputs["L1_attn_norm_w"], np.float32), (128, 4096)))
    w["onesf_in"] = np.ones((128, 128), np.float32)
    w["identf_in"] = np.eye(128, dtype=np.float32)
    w["masklo_in"] = np.where(np.tril(np.ones((128, 128), bool), -1),
                              -1e30, 0.0).astype(ml_dtypes.bfloat16)
    w["w1t_l1_b512"] = blk_rhs(np.asarray(inputs["L1_ffn_w1"], np.float32).T, 512)
    w["w2t_l1_b512"] = blk_rhs(np.asarray(inputs["L1_ffn_w2"], np.float32).T, 512)

    w["ident_bf16"] = np.eye(128, dtype=ml_dtypes.bfloat16)
    w["consts"] = f32c(np.broadcast_to(
        np.asarray(CONST_VALUES, np.float32), (128, len(CONST_VALUES))))
    w["mask_tri"] = np.where(np.triu(np.ones((128, 128), bool), 1),
                             np.float32(-1e30), np.float32(0.0)).astype(np.float32)
    return w


def kernel(**inputs):
    nc = _get_nc()
    w = _prep_weights(inputs)
    x = np.ascontiguousarray(np.asarray(inputs["L0_attn_x"], dtype=np.float32))
    in_maps = [dict(w, x=x[c]) for c in range(8)]
    res = run_bass_kernel_spmd(nc, in_maps, core_ids=list(range(8)))
    outs = np.stack([np.asarray(r["out"]) for r in res.results], axis=0)
    kernel.last_results = res.results
    return outs
